# version 05 (final)
# baseline (speedup 1.0000x reference)
import numpy as np
import ml_dtypes

import concourse.bass as bass
import concourse.mybir as mybir
from concourse.bass_utils import run_bass_kernel_spmd

F32 = mybir.dt.float32
BF16 = mybir.dt.bfloat16
AF = mybir.ActivationFunctionType
ALU = mybir.AluOpType
BF = ml_dtypes.bfloat16

N = 2048
HID = 4096
H, KVH, D = 32, 8, 128
HALF = 1024
CH = 128
THETA = 10000.0
ISQ = float(1.0 / np.sqrt(np.float32(D)))

DBG_KH = 0
O_TERMS = "all"
F_TERMS = "all"
WFT_DIRECT = False
SYNC_FULL = False
BAR_SET = ()
STAGES = ("p0", "s1", "s2q", "s2kv", "s2xa", "s2f", "s3", "s4")


def host_consts():
    t = np.arange(N, dtype=np.float64)
    inv = 1.0 / THETA ** (2.0 * np.arange(64, dtype=np.float64) / D)
    fr = t[None, :] * inv[(np.arange(128) % 64)][:, None]
    c = {}
    c["c_cos"] = np.cos(fr).astype(BF)
    c["c_sin"] = np.sin(fr).astype(BF)
    R = np.zeros((128, 128), np.float32)
    for dr in range(64):
        R[dr, dr + 64] = -1.0
    for dr in range(64, 128):
        R[dr, dr - 64] = 1.0
    c["c_rot"] = np.ascontiguousarray(R.T).astype(BF)
    c["c_id"] = np.eye(128, dtype=np.float32).astype(BF)
    c["c_onec"] = np.ones((128, 1), np.float32).astype(BF)
    c["c_oner"] = np.ones((1, 128), np.float32).astype(BF)
    c["c_halfr"] = (np.ones((1, 128), np.float32) * 0.5).astype(BF)
    s = np.arange(128)[:, None]
    tt = np.arange(128)[None, :]
    c["c_trip"] = np.where(s <= tt, 0.5, 0.0).astype(np.float32).astype(BF)
    k = np.arange(128)[:, None]
    a = np.arange(64)[None, :]
    mA = ((k >= a) & (k <= a + 64)).astype(np.float32)
    c["c_mA"] = mA.astype(BF)
    c["c_mA0"] = (mA * (k >= 64)).astype(np.float32).astype(BF)
    return c


def host_ceps():
    return np.tile(np.array([[1e-5, 1.0]], np.float32), (128, 1))


class Ring:
    def __init__(self, n, init=()):
        self.n = n
        self.free = [list(init) for _ in range(n)]
        self.i = 0

    def acquire(self):
        s = self.i % self.n
        self.i += 1
        return s, list(self.free[s])

    def release(self, s, *waits):
        self.free[s] = list(waits)


class Prog:
    def __init__(self, nc):
        self.nc = nc
        self.ops = {e: [] for e in ("sy", "po", "pe", "dv", "ac")}
        self.sems = {}
        self.cnt = {}
        self.lastw = {e: {} for e in self.ops}

    def _sem(self, name):
        if name not in self.sems:
            self.sems[name] = self.nc.ctx.enter_context(self.nc.semaphore(name))
            self.cnt[name] = 0
        return self.sems[name]

    def c(self, name):
        self._sem(name)
        return self.cnt[name]

    def op(self, eng, fn, waits=(), inc=None, n=1):
        w = []
        for s, v in waits:
            self._sem(s)
            if v > 0 and self.lastw[eng].get(s, -1) < v:
                w.append((s, v))
                self.lastw[eng][s] = v
        val = None
        if inc is not None:
            self._sem(inc)
            self.cnt[inc] += n
            val = self.cnt[inc]
        self.ops[eng].append((w, fn, inc, n))
        return val

    def dma(self, eng, sem, out, in_, waits=(), transpose=False):
        if transpose:
            fn = lambda e, o=out, i=in_: e.dma_start_transpose(o, i)
        else:
            fn = lambda e, o=out, i=in_: e.dma_start(out=o, in_=i)
        return self.op(eng, fn, waits=waits, inc=sem, n=16)

    def emit(self):
        sems = self.sems

        def runner(key):
            def run(engobj):
                for w, fn, inc, n in self.ops[key]:
                    for s, v in w:
                        engobj.wait_ge(sems[s], v)
                    ins = fn(engobj)
                    if inc is not None:
                        assert ins is not None, f"inc {inc} on op returning None"
                        ins.then_inc(sems[inc], n)
            return run

        with self.nc.Block() as blk:
            blk.sync(runner("sy"))
            blk.gpsimd(runner("po"))
            blk.tensor(runner("pe"))
            blk.vector(runner("dv"))
            blk.scalar(runner("ac"))


def build_program(stages=STAGES, dbg=False):
    nc = bass.Bass("TRN2", target_bir_lowering=False,
                   detect_race_conditions=False)
    scr_kind = "ExternalOutput" if dbg else "Internal"

    x = nc.dram_tensor("x", [N, HID], F32, kind="ExternalInput").ap()
    wq = nc.dram_tensor("wq", [HID, HID], F32, kind="ExternalInput").ap()
    wk = nc.dram_tensor("wk", [HID, KVH * D], F32, kind="ExternalInput").ap()
    wv = nc.dram_tensor("wv", [HID, KVH * D], F32, kind="ExternalInput").ap()
    wo = nc.dram_tensor("wo", [HID, HID], F32, kind="ExternalInput").ap()
    wf = nc.dram_tensor("wf", [HID, HID], F32, kind="ExternalInput").ap()
    bias = nc.dram_tensor("bias", [1, HID], F32, kind="ExternalInput").ap()
    af = nc.dram_tensor("af", [256, HID], F32, kind="ExternalInput").ap()
    bfm = nc.dram_tensor("bfm", [16, HID, 16], F32, kind="ExternalInput").ap()
    coef = nc.dram_tensor("coef", [128, 2], F32, kind="ExternalInput").ap()
    scal = nc.dram_tensor("scal", [128, 2], F32, kind="ExternalInput").ap()
    cns = {k: nc.dram_tensor(k, list(v.shape), BF16, kind="ExternalInput").ap()
           for k, v in host_consts().items()}
    ceps = nc.dram_tensor("ceps", [128, 2], F32, kind="ExternalInput").ap()
    wft_in = (nc.dram_tensor("wft_in", [HID, HID], BF16, kind="ExternalInput").ap()
              if WFT_DIRECT else None)
    out = nc.dram_tensor("out", [N, HID], F32, kind="ExternalOutput").ap()

    xn_scr = nc.dram_tensor("xn_scr", [N, HID], BF16, kind=scr_kind).ap()
    qt_scr = nc.dram_tensor("qt_scr", [16, 32, 128, CH], BF16, kind=scr_kind).ap()
    kt_scr = nc.dram_tensor("kt_scr", [128, KVH, N], BF16, kind=scr_kind).ap()
    vt_scr = nc.dram_tensor("vt_scr", [128, KVH, N], BF16, kind=scr_kind).ap()
    g_scr = nc.dram_tensor("g_scr", [N, HID], BF16, kind=scr_kind).ap()
    ot_dbg = (nc.dram_tensor("ot_dbg", [128, 32, N], BF16, kind="ExternalOutput").ap()
              if dbg else None)
    s3f_dbg = (nc.dram_tensor("s3f_dbg", [128, 1024], F32, kind="ExternalOutput").ap()
               if dbg else None)
    s3b_dbg = (nc.dram_tensor("s3b_dbg", [128, 4608], BF16, kind="ExternalOutput").ap()
               if dbg else None)
    c_dbg = (nc.dram_tensor("c_dbg", [16, 128, 8], F32, kind="ExternalOutput").ap()
             if dbg else None)
    xat_dbg = (nc.dram_tensor("xat_dbg", [2, 128, 2048], BF16, kind="ExternalOutput").ap()
               if dbg else None)
    wfb_scr = nc.dram_tensor("wfb_scr", [HID, HID], BF16, kind="Internal").ap()
    wft_scr = nc.dram_tensor("wft_scr", [HID, HID], BF16, kind="Internal").ap()
    ab_scr = nc.dram_tensor("ab_scr", [256, HID], BF16, kind="Internal").ap()
    b2_scr = nc.dram_tensor("b2_scr", [2, 128, HID], BF16, kind=scr_kind).ap()
    w_scr = nc.dram_tensor("w_scr", [2, 128], F32, kind=scr_kind).ap()

    A = nc.alloc_sbuf_tensor
    U_BIG = A("U_BIG", [128, 32 * HALF], BF16)
    U_WN = A("U_WN", [128, 32 * 512], BF16)
    U_QW = A("U_QW", [128, 12288], BF16)
    U_S4 = A("U_S4", [128, 8192], BF16)
    qcb = A("qcb", [128, 1024], BF16)
    kcb = A("kcb", [128, 1024], BF16)
    vwb = A("vwb", [128, 8 * 192], BF16)
    skw_t = A("skw", [128, 192], BF16)
    skc_t = A("skc", [128, 8 * 64], BF16)
    sst = A("sst", [128, 8 * 128], F32)
    sbf_t = A("sbf", [128, 8 * 128], BF16)
    cosb = A("cosb", [128, N], BF16)
    sinb = A("sinb", [128, N], BF16)
    rotc = A("rotc", [128, 128], BF16)
    idc = A("idc", [128, 128], BF16)
    onec = A("onec", [128, 1], BF16)
    oner = A("oner", [1, 128], BF16)
    halfr = A("halfr", [1, 128], BF16)
    trip = A("trip", [128, 128], BF16)
    mA_ = A("mA_", [128, 64], BF16)
    mA0 = A("mA0", [128, 64], BF16)
    biasr = A("biasr", [1, HID], BF16)
    wcol = A("wcol", [128, 2], F32)
    zer = A("zer", [128, 128], F32)
    s128 = A("s128", [128, 6 * 128], F32)
    s512 = A("s512", [128, 2 * 512], F32)
    b512 = A("b512", [128, 7 * 512], BF16)
    b128 = A("b128", [128, 6 * 128], BF16)
    rows = A("rows", [1, 1152], F32)
    rows2 = A("rows2", [1, 1536], F32)
    rowsb2 = A("rowsb2", [1, 1024], BF16)
    rowsb = A("rowsb", [1, 1152], BF16)
    colsx = A("colsx", [128, 8], F32)
    cbias = A("cbias", [128, 2], F32)
    est = A("est", [128, 1024], BF16)
    xat = A("xat", [128, 2 * HALF], BF16)
    xab = A("xab", [128, 256], BF16)
    assert nc.sbuf_bytes_remaining >= 0, nc.sbuf_bytes_remaining

    PB = nc.alloc_psum_tensor
    pb0 = PB("pb0", [128, 512], F32)
    pb7 = PB("pb7", [128, 512], F32)
    pb1 = PB("pb1", [128, 512], F32)
    pb2 = PB("pb2", [128, 512], F32)
    pb3 = PB("pb3", [128, 512], F32)
    pb4 = PB("pb4", [1, 512], F32)
    pb5 = PB("pb5", [128, 768], BF16)
    pb6 = PB("pb6", [128, 128], F32)
    assert nc.psum_banks_remaining >= 0

    p = Prog(nc)

    xnT = U_BIG[:, :].rearrange("p (k t) -> p k t", k=32)
    oT = xnT
    wnr = U_WN[:, :].rearrange("p (k n) -> p k n", k=32)
    wqb = [U_QW[:, 0:4096].rearrange("p (k o) -> p k o", k=32),
           U_QW[:, 4096:8192].rearrange("p (k o) -> p k o", k=32)]
    wqs = [U_QW[:, 8192:10240].bitcast(F32).rearrange("p (k o) -> p k o", k=8),
           U_QW[:, 10240:12288].bitcast(F32).rearrange("p (k o) -> p k o", k=8)]
    apri = U_QW[:, 0:8192].rearrange("p (k r) -> p k r", k=32)
    b2sl = U_S4[:, 0:1024].rearrange("p (e n) -> p e n", e=2)
    wos = [U_S4[:, 1024:2048].bitcast(F32), U_S4[:, 2048:3072].bitcast(F32)]
    gt4 = [U_S4[:, 3072:3584], U_S4[:, 3584:4096]]
    xt4 = [U_S4[:, 4096:5120].bitcast(F32), U_S4[:, 5120:6144].bitcast(F32)]
    ot4 = [U_S4[:, 6144:7168].bitcast(F32), U_S4[:, 7168:8192].bitcast(F32)]
    xs1 = U_QW[:, 0:8192].bitcast(F32)
    xnb = U_QW[:, 8192:12288]
    biasf = U_WN[0:1, 0:8192].bitcast(F32)
    b2fg = U_S4[:, 0:512].bitcast(F32)
    b2sg = U_S4[:, 512:768]
    kcv = kcb[:, :].rearrange("p (k t) -> p k t", k=8)
    vwv = vwb[:, :].rearrange("p (k t) -> p k t", k=8)
    skw = skw_t[:, :]
    skcv = skc_t[:, :].rearrange("p (k t) -> p k t", k=8)
    sstv = sst[:, :].rearrange("p (k t) -> p k t", k=8)
    sbfv = sbf_t[:, :].rearrange("p (k t) -> p k t", k=8)
    xatv = xat[:, :].rearrange("p (k t) -> p k t", k=2)

    SC = lambda i: s128[:, i * 128:(i + 1) * 128]
    S5 = lambda i: s512[:, i * 512:(i + 1) * 512]
    B5 = lambda i: b512[:, i * 512:(i + 1) * 512]
    B1 = lambda i: b128[:, i * 128:(i + 1) * 128]

    def bc_h(ap2d, nh, w):
        return ap2d.unsqueeze(1).broadcast_to([128, nh, w])

    def hv(ap2d, nh):
        return ap2d.rearrange("p (h t) -> p h t", h=nh)

    wft_w = []
    cw = 0
    cl_all = 0
    pob = 0
    pcb_d = 0
    sab_w = []
    sb2_w = []
    if "p0" in stages:
        for nm, dst in [("c_cos", cosb), ("c_sin", sinb), ("c_rot", rotc),
                        ("c_id", idc), ("c_onec", onec), ("c_oner", oner),
                        ("c_halfr", halfr), ("c_trip", trip), ("c_mA", mA_),
                        ("c_mA0", mA0)]:
            cl_all = p.dma("sy", "Lcn", dst[:, :], cns[nm])
        cb = p.dma("sy", "Lbias", biasf[:, :], bias)
        pcb_d = p.dma("sy", "Lceps", cbias[:, :], ceps)
        p.op("po", lambda e: e.memset(skc_t[:, :], 0.0))
        p.op("po", lambda e: e.memset(sst[:, :], 0.0))
        p.op("po", lambda e: e.memset(sbf_t[:, :], 0.0))
        p.op("po", lambda e: e.memset(zer[:, :], 0.0))
        pz = p.op("po", lambda e: e.memset(vwb[:, :], 0.0), inc="PO")
        pob = p.op("po", lambda e: e.tensor_copy(biasr[:, :], biasf[:, :]),
                   waits=[("Lbias", cb)], inc="PO")
        c1 = p.dma("sy", "Lcoef", colsx[:, 0:2], coef)
        c2 = p.dma("sy", "Lscal", colsx[:, 2:4], scal)
        cw = p.op("dv", lambda e: e.tensor_mul(
            wcol[:, :], colsx[:, 0:2], colsx[:, 2:4]),
            waits=[("Lcoef", c1), ("Lscal", c2)], inc="DV")


        afl = U_WN[:, 0:8192].bitcast(F32)
        abf = U_WN[:, 8192:12288]
        prev_cast = pob
        sab = 0
        for i in range(2):
            c5 = p.dma("ac", "Lafl", afl[:, :], af[128 * i:128 * (i + 1), :],
                       waits=[("PO", prev_cast)])
            prev_cast = p.op("po", lambda e: e.tensor_copy(abf[:, :], afl[:, :]),
                             waits=[("Lafl", c5), ("Sab", sab)], inc="PO")
            sab = p.dma("ac", "Sab", ab_scr[128 * i:128 * (i + 1), :], abf[:, :],
                        waits=[("PO", prev_cast)])
        sab_w = [("Sab", sab)]
        bfr = bfm.rearrange("e (ot o) r -> ot o e r", o=128)
        pv_tc = [0, 0]
        pv_dc = 0
        pv_st = 0
        pv_ld = 0
        sb2 = 0
        for ot in range(32):
            c6 = p.dma("ac", "Lb2f", b2fg[:, :].rearrange("p (e r) -> p e r", e=16),
                       bfr[ot], waits=[("PO", pv_ld)])
            pv_ld = p.op("po", lambda e: e.tensor_copy(b2sg[:, :], b2fg[:, :]),
                         waits=[("Lb2f", c6), ("PE", pv_tc[1]), ("Sb2", sb2)],
                         inc="PO")
            for hh2 in range(2):
                pv_tc[hh2] = p.op("pe", lambda e, hh2=hh2: e.transpose(
                    pb5[:, 256 * hh2:256 * hh2 + 128],
                    b2sg[:, 128 * hh2:128 * hh2 + 128], idc[:, :]),
                    waits=[("PO", pv_ld), ("DV", pv_dc), ("Lcn", cl_all)],
                    inc="PE")
                pv_dc = p.op("dv", lambda e, hh2=hh2: e.tensor_copy(
                    b2sg[:, 128 * hh2:128 * hh2 + 128],
                    pb5[:, 256 * hh2:256 * hh2 + 128]),
                    waits=[("PE", pv_tc[hh2])], inc="DV")
            sb2 = p.dma("ac", "Sb2", b2_scr[:, :, 128 * ot:128 * (ot + 1)]
                        .rearrange("e p o -> p e o"),
                        b2sg[:, :].rearrange("p (e o) -> p e o", e=2),
                        waits=[("DV", pv_dc)])
        sb2_w = [("Sb2", sb2)]
        prev_cast2 = prev_cast
        swfb = 0
        for ot in range(32):
            c7 = p.dma("ac", "Lafl", afl[:, :], wf[128 * ot:128 * (ot + 1), :],
                       waits=[("PO", prev_cast2)])
            prev_cast2 = p.op("po", lambda e: e.tensor_copy(abf[:, :], afl[:, :]),
                              waits=[("Lafl", c7), ("Swfb", swfb), ("Sab", sab)],
                              inc="PO")
            swfb = p.dma("ac", "Swfb", wfb_scr[128 * ot:128 * (ot + 1), :], abf[:, :],
                         waits=[("PO", prev_cast2)])
        swft = 0
        for kc in range(32):
            c9 = p.dma("ac", "Labf", abf[:, :], wfb_scr[:, 128 * kc:128 * (kc + 1)],
                       waits=[("Swfb", swfb), ("Swft", swft), ("PO", prev_cast2)],
                       transpose=True)
            swft = p.dma("ac", "Swft", wft_scr[128 * kc:128 * (kc + 1), :], abf[:, :],
                         waits=[("Labf", c9)])
        wft_w = [("Swft", swft)]

    pe_after_s4 = 0
    qv_q_done = {0: [], 1: []}
    qv_kv_done = {0: [], 1: []}
    qv_g_done = {0: [], 1: []}
    ac_oT_done = {0: 0, 1: 0}
    pe_s2f_done = 0
    xat_free = []
    xs1_free = []
    xnb_free = []
    xnT_w = []
    xa_done = 0
    pe_after_s2xa = 0
    po_after_s2 = 0
    s1_w = []

    for h in range(2):
        if "s1" in stages:
            for tt in range(8):
                T = h * 8 + tt
                lc = p.dma("sy", "Lx", xs1[:, :], x[128 * T:128 * (T + 1), :],
                           waits=xs1_free + ([("PE", pe_after_s2xa),
                                              ("PO", po_after_s2)]
                                             if tt == 0 else []))
                o_c = p.op("po", lambda e: e.tensor_copy(xnb[:, :], xs1[:, :]),
                           waits=[("Lx", lc)] + xnb_free, inc="PO")
                sc = p.dma("ac", "Sxn", xn_scr[128 * T:128 * (T + 1), :], xnb[:, :],
                           waits=[("PO", o_c)])
                xs1_free = [("PO", o_c)]
                xnb_free = [("Sxn", sc)]
            s1_w = [("Sxn", p.c("Sxn"))]
            for kc in range(32):
                p.dma("ac", "LxnT", xnT[:, kc, :],
                      xn_scr[HALF * h:HALF * (h + 1), 128 * kc:128 * (kc + 1)],
                      waits=s1_w + wft_w + [("PE", pe_after_s4)],
                      transpose=True)
            lxt = p.c("LxnT")
            dlast = 0
            for tc in range(2):
                msum = None
                sscnt = {}
                for K in range(32):
                    w_ = [("LxnT", lxt)] if K == 0 else []
                    if K == 0 and tc == 0:
                        w_ += [("DV", p.c("DV")), ("AC", p.c("AC"))]
                    aq = p.op("ac", lambda e, K=K, tc=tc: e.activation(
                        b512[:, 512 * (K % 2):512 * (K % 2) + 512],
                        xnT[:, K, 512 * tc:512 * (tc + 1)],
                        AF.Square, bias=0.0, scale=1.0),
                        waits=w_ + [("Lcn", cl_all)]
                        + ([("PE", sscnt[K - 2])] if K >= 2 else
                           ([("PE", p.c("PE"))] if K == 0 else [])),
                        inc="AC")
                    p.op("pe", lambda e, K=K, tc=tc: e.matmul(
                        pb4[0:1, 0:512], onec[:, :],
                        xnT[:, K, 512 * tc:512 * (tc + 1)],
                        start=(K == 0), stop=(K == 31)),
                        waits=w_ + ([("DV", dlast)] if K == 0 else []))
                    msum = p.op("pe", lambda e, K=K: e.matmul(
                        pb1[0:1, 0:512], onec[:, :],
                        b512[:, 512 * (K % 2):512 * (K % 2) + 512],
                        start=(K == 0), stop=(K == 31)),
                        waits=[("AC", aq)] + ([("DV", dlast)] if K == 0 else []),
                        inc="PE")
                    sscnt[K] = msum
                d_m = p.op("dv", lambda e: e.tensor_scalar_mul(
                    rows2[0:1, 0:512], pb4[0:1, 0:512], 1.0 / HID),
                    waits=[("PE", msum)], inc="DV")
                p.op("dv", lambda e: e.tensor_scalar_mul(
                    rows2[0:1, 512:1024], pb1[0:1, 0:512], 1.0 / HID), inc="DV")
                p.op("dv", lambda e: e.tensor_mul(
                    rows2[0:1, 1024:1536], rows2[0:1, 0:512], rows2[0:1, 0:512]),
                    inc="DV")
                d_v = p.op("dv", lambda e: e.tensor_sub(
                    rows2[0:1, 512:1024], rows2[0:1, 512:1024],
                    rows2[0:1, 1024:1536]), inc="DV")
                a_l = p.op("ac", lambda e: e.activation(
                    rows2[0:1, 1024:1536], rows2[0:1, 512:1024], AF.Ln,
                    bias=cbias[0:1, 0:1], scale=1.0),
                    waits=[("DV", d_v), ("Lceps", pcb_d)], inc="AC")
                a_r = p.op("ac", lambda e: e.activation(
                    rows2[0:1, 512:1024], rows2[0:1, 1024:1536], AF.Exp,
                    bias=0.0, scale=-0.5), inc="AC")
                d_mr = p.op("dv", lambda e: e.scalar_tensor_tensor(
                    rows2[0:1, 1024:1536], rows2[0:1, 0:512], -1.0,
                    rows2[0:1, 512:1024], ALU.mult, ALU.mult),
                    waits=[("AC", a_r)], inc="DV")
                a_c1 = p.op("ac", lambda e: e.copy(
                    rowsb2[0:1, 0:512], rows2[0:1, 512:1024]),
                    waits=[("DV", d_mr)], inc="AC")
                a_c2 = p.op("ac", lambda e: e.copy(
                    rowsb2[0:1, 512:1024], rows2[0:1, 1024:1536]), inc="AC")
                m_b1 = p.op("pe", lambda e: e.matmul(
                    pb2[:, 0:512], oner[:, :], rowsb2[0:1, 0:512],
                    start=True, stop=True),
                    waits=[("AC", a_c1), ("DV", dlast)], inc="PE")
                m_b2 = p.op("pe", lambda e: e.matmul(
                    pb3[:, 0:512], oner[:, :], rowsb2[0:1, 512:1024],
                    start=True, stop=True), waits=[("AC", a_c2)], inc="PE")
                for K in range(32):
                    d_t = p.op("dv", lambda e, K=K, tc=tc: e.tensor_mul(
                        s512[:, 0:512], xnT[:, K, 512 * tc:512 * (tc + 1)],
                        pb2[:, 0:512]),
                        waits=[("PE", m_b2)], inc="DV")
                    dlast = p.op("dv", lambda e, K=K, tc=tc: e.tensor_add(
                        xnT[:, K, 512 * tc:512 * (tc + 1)], s512[:, 0:512],
                        pb3[:, 0:512]), inc="DV")
            xnT_w = [("DV", dlast)]

        wqb_ring = Ring(2, init=[("PE", pe_after_s4), ("AC", p.c("AC")),
                                 ("DV", p.c("DV"))])
        wqs_ring = Ring(2, init=xnT_w)
        st2_ring = Ring(2, init=[("Sst0", p.c("Sst0"))] if h else [])
        pbQ_ring = Ring(2, init=[("DV", p.c("DV")), ("AC", p.c("AC"))])

        def gemm_wstat(wsrc, n_of, store_fn):
            wsr = wsrc.rearrange("(kc p) o -> p kc o", p=128)
            for of in range(n_of):
                bslot, bw = wqb_ring.acquire()
                castc = []
                for pc4 in range(4):
                    sslot, sw = wqs_ring.acquire()
                    lc = p.dma("sy", f"Lwqs{sslot}", wqs[sslot][:, :, :],
                               wsr[:, 8 * pc4:8 * (pc4 + 1),
                                   128 * of:128 * (of + 1)], waits=sw)
                    cc = p.op("po", lambda e, sslot=sslot, bslot=bslot, pc4=pc4:
                              e.tensor_copy(wqb[bslot][:, 8 * pc4:8 * (pc4 + 1), :],
                                            wqs[sslot][:, :, :]),
                              waits=[(f"Lwqs{sslot}", lc)] + (bw if pc4 == 0 else []),
                              inc="PO")
                    wqs_ring.release(sslot, ("PO", cc))
                    castc.append(cc)
                mml = None
                for tc in range(2):
                    pslot, pw = pbQ_ring.acquire()
                    pbk = pb0 if pslot == 0 else pb7
                    for K in range(32):
                        w_ = [("PO", castc[K // 8])]
                        if K == 0:
                            w_ += pw + xnT_w
                        mml = p.op("pe", lambda e, pbk=pbk, bslot=bslot, K=K, tc=tc:
                                   e.matmul(pbk[:, :], wqb[bslot][:, K, :],
                                            xnT[:, K, 512 * tc:512 * (tc + 1)],
                                            start=(K == 0), stop=(K == 31)),
                                   waits=w_, inc=("PE" if K == 31 else None))
                    eslot, ew = st2_ring.acquire()
                    sl = est[:, 512 * eslot:512 * (eslot + 1)]
                    dc = p.op("dv", lambda e, sl=sl, pbk=pbk: e.tensor_copy(sl, pbk[:, :]),
                              waits=[("PE", mml)] + ew, inc="DV")
                    pbQ_ring.release(pslot, ("DV", dc))
                    stc = store_fn(of, tc, sl, dc, eslot)
                    st2_ring.release(eslot, (f"Sst{eslot}", stc))
                wqb_ring.release(bslot, ("PE", mml))

        if "s2q" in stages:
            def q_store(of, tc, sl, dc, eslot):
                cbase = h * 8 + tc * 4
                return p.dma("ac", f"Sst{eslot}",
                             qt_scr[cbase:cbase + 4, of].rearrange("c p t -> p c t"),
                             sl.rearrange("p (c t) -> p c t", c=4),
                             waits=[("DV", dc)])
            gemm_wstat(wq, 32, q_store)
            qv_q_done[h] = [("Sst0", p.c("Sst0")), ("Sst1", p.c("Sst1"))]

        if "s2kv" in stages:
            def kv_store(scr):
                def fn(of, tc, sl, dc, eslot):
                    tb = HALF * h + 512 * tc
                    return p.dma("ac", f"Sst{eslot}", scr[:, of, tb:tb + 512], sl,
                                 waits=[("DV", dc)])
                return fn
            gemm_wstat(wk, 8, kv_store(kt_scr))
            gemm_wstat(wv, 8, kv_store(vt_scr))
            qv_kv_done[h] = [("Sst0", p.c("Sst0")), ("Sst1", p.c("Sst1"))]

        if "s2xa" in stages:
            alast = 0
            for kc in range(32):
                alast = p.dma("sy", "Lapri", apri[:, kc, :],
                              ab_scr[:, 128 * kc:128 * (kc + 1)],
                              waits=(([("PE", p.c("PE")), ("PO", p.c("PO"))] + sab_w)
                                     if kc == 0 else []),
                              transpose=True)
            apri_w = [("Lapri", alast)]
            pbxa_free = [("DV", p.c("DV"))]
            pv_tp = [0, 0]
            for tt in range(8):
                mml = None
                for K in range(32):
                    w_ = []
                    if K == 0:
                        w_ = apri_w + pbxa_free + xat_free
                    mml = p.op("pe", lambda e, K=K, tt=tt: e.matmul(
                        pb2[:, 0:256], xnT[:, K, 128 * tt:128 * (tt + 1)],
                        apri[:, K, :], start=(K == 0), stop=(K == 31)),
                        waits=w_, inc=("PE" if K == 31 else None))
                dc = p.op("dv", lambda e: e.tensor_copy(xab[:, :], pb2[:, 0:256]),
                          waits=[("PE", mml), ("AC", p.c("AC"))], inc="DV")
                pbxa_free = [("DV", dc)]
                for hh2 in range(2):
                    pv_tp[hh2] = p.op("pe", lambda e, hh2=hh2: e.transpose(
                        pb5[:, 256 * hh2:256 * hh2 + 128],
                        xab[:, 128 * hh2:128 * hh2 + 128], idc[:, :]),
                        waits=[("DV", dc), ("AC", p.c("AC"))], inc="PE")
                    p.op("ac", lambda e, hh2=hh2, tt=tt: e.activation(
                        xatv[:, hh2, 128 * tt:128 * (tt + 1)],
                        pb5[:, 256 * hh2:256 * hh2 + 128], AF.Identity,
                        bias=0.0, scale=wcol[:, hh2:hh2 + 1]),
                        waits=[("PE", pv_tp[hh2]), ("DV", cw)], inc="AC")
            xa_done = p.c("AC")
            if dbg:
                p.dma("ac", "Sxa", xat_dbg[h], xat[:, :], waits=[("AC", xa_done)])
        pe_after_s2xa = p.c("PE")
        po_after_s2 = p.c("PO")

        if "s2f" in stages:
            pbF_ring = Ring(2, init=[("DV", p.c("DV")), ("AC", p.c("AC"))])
            gst_ring = st2_ring
            b2c = p.dma("sy", "Lb2s", b2sl, b2_scr[:, :, 0:512].rearrange("e p o -> p e o"),
                        waits=[("PE", pe_after_s4)] + wft_w + sb2_w)
            fills = {}
            relw = {}
            wftr = (wft_in if WFT_DIRECT else wft_scr).rearrange(
                "(kc p) o -> p kc o", p=128)
            wn_free_j0 = [("PE", pe_after_s4)] + wft_w
            for j in range(8):
                for g4 in range(4):
                    fw = wn_free_j0 if j == 0 else relw.get(g4, [])
                    fills[(j, g4)] = p.dma(
                        "sy", f"Lwn{g4}",
                        wnr[:, 8 * g4:8 * (g4 + 1), :],
                        wftr[:, 8 * g4:8 * (g4 + 1), 512 * j:512 * (j + 1)],
                        waits=fw)
                for tt in range(8):
                    pslot, pw = pbF_ring.acquire()
                    pbk = pb0 if pslot == 0 else pb7
                    mml = None
                    for K in range(32):
                        w_ = ([(f"Lwn{K // 8}", fills[(j, K // 8)])]
                              if K % 8 == 0 else [])
                        if K == 0:
                            w_ = w_ + pw + [("AC", xa_done)] + xnT_w
                        last_tt = (tt == 7 and K % 8 == 7)
                        mml = p.op("pe", lambda e, pbk=pbk, K=K, tt=tt: e.matmul(
                            pbk[:, :], xnT[:, K, 128 * tt:128 * (tt + 1)],
                            wnr[:, K, :], start=(K == 0), stop=False),
                            waits=w_, inc=("PE" if last_tt else None))
                        if last_tt:
                            relw[K // 8] = [("PE", mml)]
                    if F_TERMS == "all":
                        p.op("pe", lambda e, pbk=pbk, j=j: e.matmul(
                            pbk[:, :], oner[:, :], biasr[0:1, 512 * j:512 * (j + 1)],
                            start=False, stop=False),
                            waits=[("PO", pob), ("Lcn", cl_all)])
                        for kc2 in range(2):
                            mml = p.op("pe", lambda e, pbk=pbk, kc2=kc2, tt=tt: e.matmul(
                                pbk[:, :], xatv[:, kc2, 128 * tt:128 * (tt + 1)],
                                b2sl[:, kc2, :], start=False, stop=(kc2 == 1)),
                                waits=[("Lb2s", b2c)], inc=("PE" if kc2 == 1 else None))
                    else:
                        mml = p.op("pe", lambda e, pbk=pbk: e.matmul(
                            pbk[:, :], oner[:, :], biasr[0:1, 0:512],
                            start=False, stop=True),
                            waits=[("PO", pob), ("Lcn", cl_all)], inc="PE")
                    gslot2, gw2 = gst_ring.acquire()
                    gsl = est[:, 512 * gslot2:512 * (gslot2 + 1)]
                    a5 = p.op("ac", lambda e, pbk=pbk: e.activation(
                        s512[:, 512:1024], pbk[:, :], AF.Sigmoid, bias=0.0,
                        scale=1.0),
                        waits=[("PE", mml), ("DV", p.c("DV"))], inc="AC")
                    a5 = p.op("ac", lambda e, pbk=pbk: e.activation(
                        s512[:, 0:512], pbk[:, :], AF.Identity, bias=0.0,
                        scale=1.0), inc="AC")
                    d5f = p.op("dv", lambda e, gsl=gsl: e.tensor_mul(
                        gsl, s512[:, 0:512], s512[:, 512:1024]),
                        waits=[("AC", a5)] + gw2, inc="DV")
                    pbF_ring.release(pslot, ("DV", d5f))
                    T = h * 8 + tt
                    gc = p.dma("ac", f"Sst{gslot2}",
                               g_scr[128 * T:128 * (T + 1), 512 * j:512 * (j + 1)],
                               gsl, waits=[("DV", d5f)])
                    gst_ring.release(gslot2, (f"Sst{gslot2}", gc))
                if j < 7:
                    b2c = p.dma("sy", "Lb2s", b2sl,
                                b2_scr[:, :, 512 * (j + 1):512 * (j + 2)]
                                .rearrange("e p o -> p e o"), waits=[("PE", mml)])
            qv_g_done[h] = [("Sst0", p.c("Sst0")), ("Sst1", p.c("Sst1"))]
            pe_s2f_done = p.c("PE")
            xat_free = [("PE", pe_s2f_done)]

        if "s3" in stages:
            qc_ring = Ring(2)
            kc_free = []
            vw_free = []
            pbO_ring = Ring(2, init=[("AC", p.c("AC")), ("DV", p.c("DV"))])
            pv = dict(d1=0, d5=0, d6=0, d9=0, d11=0, d16=0, d18=0, d21=0,
                      d22=0, d23=0, o3=0, o1=0, d7=0, a11=0, m1=0, m9=0)
            for cc in range(8):
                c = h * 8 + cc
                ct = CH * c
                kload = p.dma("sy", "Lkc", kcb[:, :].rearrange("p (k t) -> p k t", k=8),
                              kt_scr[:, :, ct:ct + CH],
                              waits=kc_free + qv_kv_done[h])
                if c == 0:
                    vload = p.dma("sy", "Lvw", vwv[:, :, 64:192], vt_scr[:, :, 0:CH],
                                  waits=vw_free + qv_kv_done[h] + [("PO", pz)])
                else:
                    vload = p.dma("sy", "Lvw", vwv[:, :, :], vt_scr[:, :, ct - 64:ct + CH],
                                  waits=vw_free + qv_kv_done[h])
                def emit_qload(kh2):
                    qslot, qw = qc_ring.acquire()
                    ql = p.dma("sy", f"Lqc{qslot}",
                               qcb[:, 512 * qslot:512 * (qslot + 1)]
                               .rearrange("p (h t) -> p h t", h=4),
                               qt_scr[c, 4 * kh2:4 * kh2 + 4]
                               .rearrange("h p t -> p h t"),
                               waits=qw + qv_q_done[h])
                    qloads.append((qslot, ql))
                qloads = []
                emit_qload(0)
                emit_qload(1)

                dbg_w = []
                for kh in range(KVH):
                    if SYNC_FULL:
                        bar = [("PE", p.c("PE")), ("DV", p.c("DV")),
                               ("AC", p.c("AC")), ("PO", p.c("PO"))]
                        for s_, v_ in bar:
                            if v_ > 0:
                                pass
                        dbg_w = list(set(dbg_w))
                    else:
                        bar = []
                    kk = kcv[:, kh, :]
                    qslot, ql = qloads[kh]
                    qcv = qcb[:, 512 * qslot:512 * (qslot + 1)]
                    a0 = p.op("ac", lambda e, kk=kk: e.activation(
                        SC(0), kk, AF.Exp, bias=0.0, scale=1.0),
                        waits=[("Lkc", kload), ("PO", pv["o3"]), ("DV", pv["d1"]),
                               ("PE", pv["m1"]), ("Lceps", pcb_d)] + dbg_w,
                        inc="AC")
                    a1 = p.op("ac", lambda e, kk=kk: e.activation(
                        SC(1), kk, AF.Exp, bias=0.0, scale=-1.0),
                        waits=[("DV", pv["d23"]), ("DV", pv["o1"])] + dbg_w,
                        inc="AC")
                    o1 = p.op("dv", lambda e: e.tensor_tensor(
                        SC(1), SC(0), SC(1), ALU.min),
                        waits=[("AC", a1)] + (bar if "po" in BAR_SET else []),
                        inc="DV")
                    a2 = p.op("ac", lambda e: e.activation(
                        SC(2), SC(1), AF.Ln, bias=cbias[:, 1:2], scale=1.0),
                        waits=[("DV", o1)], inc="AC")
                    o2 = p.op("po", lambda e, kk=kk: e.tensor_scalar_min(
                        SC(3), kk, 0.0),
                        waits=[("DV", pv["d11"]), ("Lkc", kload)], inc="PO")
                    o3 = p.op("po", lambda e: e.tensor_sub(SC(0), SC(3), SC(2)),
                              waits=[("AC", a2)], inc="PO")
                    d1 = p.op("dv", lambda e: e.tensor_tensor_scan(
                        SC(4), SC(0), zer[:, :], 0.0, ALU.add, ALU.add),
                        waits=[("PO", o3)] + dbg_w
                        + (bar if "dv" in BAR_SET else []), inc="DV")
                    a3 = p.op("ac", lambda e: e.activation(
                        SC(5), SC(4), AF.Exp, bias=0.0, scale=1.0 / 16.0),
                        waits=[("DV", d1)], inc="AC")
                    a4 = p.op("ac", lambda e: e.activation(
                        SC(2), SC(4), AF.Exp, bias=0.0, scale=-1.0 / 16.0),
                        waits=[("PO", o3)], inc="AC")
                    a5 = p.op("ac", lambda e: e.activation(
                        colsx[:, 0:1], SC(4)[:, 127:128], AF.Exp, bias=0.0,
                        scale=1.0 / 16.0), waits=[("DV", pv["d23"])], inc="AC")
                    a6 = p.op("ac", lambda e, kk=kk: e.activation(
                        B1(0), kk, AF.Exp, bias=0.0, scale=1.0),
                        waits=[("Lkc", kload), ("PE", pv["m1"])], inc="AC")
                    m1 = p.op("pe", lambda e: e.matmul(
                        pb4[0:1, 0:128], onec[:, :], B1(0), start=True, stop=True),
                        waits=[("AC", a6), ("DV", pv["d21"]), ("Lcn", cl_all)]
                        + (bar if "pe" in BAR_SET else []), inc="PE")
                    d2 = p.op("dv", lambda e: e.reciprocal(rows[0:1, 0:128],
                                                           pb4[0:1, 0:128]),
                              waits=[("PE", m1)], inc="DV")
                    a7 = p.op("ac", lambda e: e.copy(rowsb[0:1, 0:128],
                                                     rows[0:1, 0:128]),
                              waits=[("DV", d2)], inc="AC")
                    m2 = p.op("pe", lambda e: e.matmul(
                        pb3[:, 0:128], oner[:, :], rowsb[0:1, 0:128],
                        start=True, stop=True),
                        waits=[("AC", a7), ("DV", pv["d22"])], inc="PE")
                    d3 = p.op("dv", lambda e: e.tensor_mul(SC(3), B1(0), SC(2)),
                              waits=[("AC", a4)], inc="DV")
                    d4 = p.op("dv", lambda e: e.tensor_mul(B1(1), SC(3), pb3[:, 0:128]),
                              waits=[("PE", m2)], inc="DV")
                    a8 = p.op("ac", lambda e: e.activation(
                        B1(2), B1(1), AF.Identity, bias=0.0, scale=colsx[:, 0:1]),
                        waits=[("DV", d4), ("AC", a5)], inc="AC")
                    t1 = p.op("pe", lambda e: e.transpose(pb5[:, 0:128], B1(2), idc[:, :]),
                              waits=[("AC", a8), ("DV", pv["d7"])], inc="PE")
                    d5 = p.op("dv", lambda e: e.tensor_copy(B1(3), pb5[:, 0:128]),
                              waits=[("PE", t1)], inc="DV")
                    t2 = p.op("pe", lambda e, kh=kh: e.transpose(
                        pb5[:, 256:384], vwv[:, kh, 0:128], idc[:, :]),
                        waits=[("Lvw", vload), ("DV", d5)], inc="PE")
                    d6 = p.op("dv", lambda e: e.tensor_copy(B1(4), pb5[:, 256:384]),
                              waits=[("PE", t2)], inc="DV")
                    t3 = p.op("pe", lambda e, kh=kh: e.transpose(
                        pb5[:, 384:512], vwv[:, kh, 64:192], idc[:, :]),
                        waits=[("DV", d6)], inc="PE")
                    d7 = p.op("dv", lambda e: e.tensor_copy(B1(5), pb5[:, 384:512]),
                              waits=[("PE", t3)], inc="DV")
                    d8 = p.op("dv", lambda e, kh=kh: e.tensor_copy(
                        skw[:, 0:64], skcv[:, kh, :]),
                        waits=[("PE", pv["m9"])], inc="DV")
                    m3 = p.op("pe", lambda e, kk=kk: e.matmul(
                        pb1[:, 0:128], rotc[:, :], kk, start=True, stop=True),
                        waits=[("DV", pv["d18"]), ("Lkc", kload)], inc="PE")
                    d9 = p.op("dv", lambda e, kk=kk, ct=ct: e.tensor_mul(
                        SC(3), kk, cosb[:, ct:ct + CH]), inc="DV")
                    d10 = p.op("dv", lambda e, ct=ct: e.tensor_mul(
                        SC(2), pb1[:, 0:128], sinb[:, ct:ct + CH]),
                        waits=[("PE", m3)], inc="DV")
                    d11 = p.op("dv", lambda e: e.tensor_add(skw[:, 64:192], SC(3), SC(2)),
                               waits=[("DV", d8)], inc="DV")
                    a9 = p.op("ac", lambda e, qcv=qcv: e.activation(
                        B5(0), qcv, AF.Exp, bias=0.0, scale=1.0),
                        waits=[(f"Lqc{qslot}", ql), ("PE", pv["m9"])], inc="AC")
                    m4 = p.op("pe", lambda e: e.matmul(
                        pb4[0:1, 0:512], onec[:, :], B5(0), start=True, stop=True),
                        waits=[("AC", a9), ("DV", d2)], inc="PE")
                    d12 = p.op("dv", lambda e: e.reciprocal(rows[0:1, 128:640],
                                                            pb4[0:1, 0:512]),
                               waits=[("PE", m4)], inc="DV")
                    a10 = p.op("ac", lambda e: e.copy(rowsb[0:1, 128:640],
                                                      rows[0:1, 128:640]),
                               waits=[("DV", d12)], inc="AC")
                    m5 = p.op("pe", lambda e: e.matmul(
                        pb3[:, 0:512], oner[:, :], rowsb[0:1, 128:640],
                        start=True, stop=True), waits=[("AC", a10), ("DV", d4)],
                        inc="PE")
                    d13 = p.op("dv", lambda e: e.tensor_tensor(
                        hv(S5(0), 4), hv(B5(0), 4), bc_h(SC(5), 4, 128), ALU.mult),
                        waits=[("AC", a3)], inc="DV")
                    d14 = p.op("dv", lambda e: e.tensor_mul(B5(1), S5(0), pb3[:, 0:512]),
                               waits=[("PE", m5), ("DV", d13)], inc="DV")
                    m6 = p.op("pe", lambda e, qcv=qcv: e.matmul(
                        pb1[:, 0:512], rotc[:, :], qcv, start=True, stop=True),
                        waits=[("DV", d10), (f"Lqc{qslot}", ql)], inc="PE")
                    d15 = p.op("dv", lambda e, qcv=qcv, ct=ct: e.tensor_tensor(
                        hv(S5(0), 4), hv(qcv, 4), bc_h(cosb[:, ct:ct + CH], 4, CH),
                        ALU.mult), waits=[("DV", d14)], inc="DV")
                    d16 = p.op("dv", lambda e, ct=ct: e.tensor_tensor(
                        hv(S5(1), 4), hv(pb1[:, 0:512], 4),
                        bc_h(sinb[:, ct:ct + CH], 4, CH), ALU.mult),
                        waits=[("PE", m6)], inc="DV")
                    d17 = p.op("dv", lambda e: e.tensor_add(B5(2), S5(0), S5(1)),
                               inc="DV")
                    m7 = p.op("pe", lambda e: e.matmul(
                        pb1[:, 0:512], B1(1), B5(1), start=True, stop=True),
                        waits=[("DV", d16), ("DV", d14)], inc="PE")
                    d18 = p.op("dv", lambda e: e.tensor_tensor(
                        hv(B5(3), 4), hv(pb1[:, 0:512], 4), bc_h(trip[:, :], 4, 128),
                        ALU.mult), waits=[("PE", m7)], inc="DV")
                    m8 = p.op("pe", lambda e: e.matmul(
                        pb2[:, 0:256], skw[:, 0:128],
                        hv(B5(2), 4)[:, :, 0:64], start=True, stop=True),
                        waits=[("DV", d17), ("DV", d11), ("AC", pv["a11"])],
                        inc="PE")
                    m9 = p.op("pe", lambda e: e.matmul(
                        pb2[:, 256:512], skw[:, 64:192],
                        hv(B5(2), 4)[:, :, 64:128], start=True, stop=True),
                        inc="PE")
                    a11 = p.op("ac", lambda e: e.activation(
                        B5(4), pb2[:, 0:512], AF.Exp, bias=0.0, scale=ISQ),
                        waits=[("PE", m9)], inc="AC")
                    mk0 = mA0 if c == 0 else mA_
                    d19 = p.op("dv", lambda e, mk0=mk0: e.tensor_tensor(
                        hv(B5(5)[:, 0:256], 4), hv(B5(4)[:, 0:256], 4),
                        bc_h(mk0[:, :], 4, 64), ALU.mult),
                        waits=[("AC", a11)], inc="DV")
                    d20 = p.op("dv", lambda e: e.tensor_tensor(
                        hv(B5(5)[:, 256:512], 4), hv(B5(4)[:, 256:512], 4),
                        bc_h(mA_[:, :], 4, 64), ALU.mult), inc="DV")
                    m10 = p.op("pe", lambda e: e.matmul(
                        pb4[0:1, 0:256], onec[:, :], B5(5)[:, 0:256],
                        start=True, stop=True), waits=[("DV", d19), ("DV", d12)],
                        inc="PE")
                    m11 = p.op("pe", lambda e: e.matmul(
                        pb4[0:1, 256:512], onec[:, :], B5(5)[:, 256:512],
                        start=True, stop=True), waits=[("DV", d20)], inc="PE")
                    d21 = p.op("dv", lambda e: e.reciprocal(rows[0:1, 640:1152],
                                                            pb4[0:1, 0:512]),
                               waits=[("PE", m11)], inc="DV")
                    a12 = p.op("ac", lambda e: e.copy(rowsb[0:1, 640:1152],
                                                      rows[0:1, 640:1152]),
                               waits=[("DV", d21)], inc="AC")
                    m12 = p.op("pe", lambda e: e.matmul(
                        pb3[:, 0:512], halfr[:, :], rowsb[0:1, 640:1152],
                        start=True, stop=True), waits=[("AC", a12), ("DV", d14)],
                        inc="PE")
                    d22 = p.op("dv", lambda e: e.tensor_mul(B5(6), B5(5), pb3[:, 0:512]),
                               waits=[("PE", m12)], inc="DV")
                    pslot, pw = pbO_ring.acquire()
                    pbO = pb0 if pslot == 0 else pb7
                    terms = []
                    if O_TERMS in ("all", "inter"):
                        terms.append(("inter", None))
                    if O_TERMS in ("all", "intra"):
                        terms.append(("intra", None))
                    if O_TERMS in ("all", "swa"):
                        terms += [("pv0", hh3) for hh3 in range(4)]
                        terms += [("pv1", hh3) for hh3 in range(4)]
                    m13 = None
                    for ti, (tname, hh3) in enumerate(terms):
                        st = (ti == 0)
                        sp = (ti == len(terms) - 1)
                        w_ = pw if ti == 0 else []
                        if tname == "inter":
                            m13 = p.op("pe", lambda e, pbO=pbO, kh=kh, st=st, sp=sp:
                                       e.matmul(pbO[:, :], sbfv[:, kh, :], B5(1),
                                                start=st, stop=sp),
                                       waits=[("DV", d14)] + w_,
                                       inc=("PE" if sp else None))
                        elif tname == "intra":
                            m13 = p.op("pe", lambda e, pbO=pbO, st=st, sp=sp:
                                       e.matmul(pbO[:, :], B1(5), B5(3),
                                                start=st, stop=sp),
                                       waits=[("DV", d18), ("DV", d7)] + w_,
                                       inc=("PE" if sp else None))
                        elif tname == "pv0":
                            m13 = p.op("pe", lambda e, pbO=pbO, hh3=hh3, st=st, sp=sp:
                                       e.matmul(pbO[:, 128 * hh3:128 * hh3 + 64],
                                                B1(4), B5(6)[:, 64 * hh3:64 * hh3 + 64],
                                                start=st, stop=sp),
                                       waits=[("DV", d22)] + w_,
                                       inc=("PE" if sp else None))
                        else:
                            m13 = p.op("pe", lambda e, pbO=pbO, hh3=hh3, st=st, sp=sp:
                                       e.matmul(pbO[:, 128 * hh3 + 64:128 * hh3 + 128],
                                                B1(5),
                                                B5(6)[:, 256 + 64 * hh3:256 + 64 * hh3 + 64],
                                                start=st, stop=sp),
                                       waits=w_, inc=("PE" if sp else None))
                    a13 = p.op("ac", lambda e, pbO=pbO, kh=kh, cc=cc: e.activation(
                        oT[:, 4 * kh:4 * kh + 4, CH * cc:CH * (cc + 1)],
                        hv(pbO[:, :], 4), AF.Identity, bias=0.0, scale=1.0),
                        waits=[("PE", m13)], inc="AC")
                    if dbg:
                        p.dma("ac", "Sod",
                              ot_dbg[:, 4 * kh:4 * kh + 4, ct:ct + CH],
                              oT[:, 4 * kh:4 * kh + 4, CH * cc:CH * (cc + 1)],
                              waits=[("AC", a13)])
                    pbO_ring.release(pslot, ("AC", a13))
                    m14 = p.op("pe", lambda e: e.matmul(
                        pb6[:, :], B1(3), B1(5), start=True, stop=True),
                        waits=[("DV", d5), ("DV", d7), ("DV", pv["d23"])], inc="PE")
                    d23 = p.op("dv", lambda e, kh=kh: e.scalar_tensor_tensor(
                        sstv[:, kh, :], sstv[:, kh, :], colsx[:, 0:1], pb6[:, :],
                        ALU.mult, ALU.add), waits=[("PE", m14), ("AC", a13)],
                        inc="DV")
                    a14 = p.op("ac", lambda e, kh=kh: e.activation(
                        sbfv[:, kh, :], sstv[:, kh, :], AF.Identity, bias=0.0,
                        scale=0.5), waits=[("DV", d23)], inc="AC")
                    d24 = p.op("dv", lambda e, kh=kh: e.tensor_copy(
                        skcv[:, kh, :], skw[:, 128:192]),
                        waits=[("PE", m9)], inc="DV")
                    if dbg and c == 0 and kh == DBG_KH:
                        p.dma("ac", "Sd3", s3f_dbg[:, 0:128], SC(4),
                              waits=[("DV", d23)])
                        p.dma("ac", "Sd3", s3f_dbg[:, 128:640], S5(1),
                              waits=[("DV", d23)])
                        for off, src in [(0, B5(0)), (512, B5(1)), (1024, B5(2)),
                                         (1536, B5(3)), (2048, B5(4)),
                                         (2560, B5(6))]:
                            p.dma("ac", "Sd3", s3b_dbg[:, 3 * 0 + off:off + 512],
                                  src, waits=[("DV", d23)])
                        for off, src in [(3072, B1(1)), (3200, B1(2)),
                                         (3328, B1(3)), (3456, B1(4)),
                                         (3584, B1(5)), (3712, B1(0))]:
                            p.dma("ac", "Sd3", s3b_dbg[:, off:off + 128],
                                  src, waits=[("DV", d23)])
                        p.dma("ac", "Sd3", s3b_dbg[:, 3840:4032],
                              skw[:, :], waits=[("DV", d23)])
                        p.dma("ac", "Sd3", s3b_dbg[:, 4032:4544],
                              qcv, waits=[("DV", d23)])
                        dbg_w = [("Sd3", p.c("Sd3"))]
                    qc_ring.release(qslot, ("DV", d15), ("PE", m6), ("AC", a9),
                                    *dbg_w)
                    if kh + 2 < KVH:
                        emit_qload(kh + 2)
                    pv = dict(d1=d1, d5=d5, d6=d6, d9=d9, d11=d11, d16=d16,
                              d18=d18, d21=d21, d22=d22, d23=d23, o3=o3, o1=o1, d7=d7,
                              a11=a11, m1=m1, m9=m9)
                kc_free = [("DV", p.c("DV")), ("AC", p.c("AC")), ("PO", p.c("PO")),
                           ("PE", p.c("PE"))]
                vw_free = [("PE", p.c("PE"))]
            ac_oT_done[h] = p.c("AC")

        if "s4" in stages:
            wor = wo.rearrange("(kc p) o -> p kc o", p=128)
            wos_ring = Ring(2, init=[("PE", p.c("PE"))])
            pbS_ring = Ring(2, init=[("AC", p.c("AC")), ("DV", p.c("DV"))])
            gx_ring = Ring(2)
            ot_ring = Ring(2)
            fillfree = {}
            for j in range(8):
                castc = {}
                for K in range(32):
                    sslot, sw = wos_ring.acquire()
                    lc = p.dma("sy", f"Lwos{sslot}", wos[sslot][:, :],
                               wor[:, K, 512 * j:512 * (j + 1)], waits=sw)
                    g4w = ([("PE", pe_s2f_done)] if j == 0
                           else [("PE", fillfree[K])])
                    cc = p.op("po", lambda e, sslot=sslot, K=K: e.tensor_copy(
                        wnr[:, K, :], wos[sslot][:, :]),
                        waits=[(f"Lwos{sslot}", lc)] + g4w, inc="PO")
                    wos_ring.release(sslot, ("PO", cc))
                    castc[K] = cc
                for tt in range(8):
                    T = h * 8 + tt
                    pslot, pw = pbS_ring.acquire()
                    pbk = pb0 if pslot == 0 else pb7
                    mml = None
                    grp_end = {}
                    for K in range(32):
                        w_ = [("PO", castc[K])]
                        if K == 0:
                            w_ += pw + [("AC", ac_oT_done[h])]
                        do_inc = (K == 31) or (tt == 7 and K % 8 == 7)
                        mml = p.op("pe", lambda e, pbk=pbk, K=K, tt=tt: e.matmul(
                            pbk[:, :], oT[:, K, CH * tt:CH * (tt + 1)],
                            wnr[:, K, :], start=(K == 0), stop=(K == 31)),
                            waits=w_, inc=("PE" if do_inc else None))
                        if tt == 7 and K % 8 == 7:
                            for kk2 in range(K - 7, K + 1):
                                grp_end[kk2] = mml
                    if tt == 7:
                        fillfree = grp_end
                    gslot, gw = gx_ring.acquire()
                    gl = p.dma("sy", f"Lgt{gslot}", gt4[gslot][:, :],
                               g_scr[128 * T:128 * (T + 1), 512 * j:512 * (j + 1)],
                               waits=gw + qv_g_done[h])
                    xl = p.dma("sy", f"Lxt{gslot}", xt4[gslot][:, :],
                               x[128 * T:128 * (T + 1), 512 * j:512 * (j + 1)])
                    oslot, ow = ot_ring.acquire()
                    d25 = p.op("dv", lambda e, pbk=pbk, gslot=gslot:
                               e.scalar_tensor_tensor(
                                   S5(0), pbk[:, :], 1.0, gt4[gslot][:, :],
                                   ALU.mult, ALU.mult),
                               waits=[("PE", mml), (f"Lgt{gslot}", gl)], inc="DV")
                    d26 = p.op("dv", lambda e, gslot=gslot, oslot=oslot: e.tensor_add(
                        ot4[oslot][:, :], S5(0), xt4[gslot][:, :]),
                        waits=[(f"Lxt{gslot}", xl)] + ow, inc="DV")
                    pbS_ring.release(pslot, ("DV", d25))
                    gx_ring.release(gslot, ("DV", d26))
                    oc = p.dma("ac", f"So{oslot}",
                               out[128 * T:128 * (T + 1), 512 * j:512 * (j + 1)],
                               ot4[oslot][:, :], waits=[("DV", d26)])
                    ot_ring.release(oslot, (f"So{oslot}", oc))
            pe_after_s4 = p.c("PE")

    fin = [(s, p.c(s)) for s in list(p.cnt.keys())
           if s[0] in ("L", "S") and p.cnt[s] > 0]
    p.op("sy", lambda e: e.nop(), waits=fin)
    p.emit()
    return nc


_BUILD_CACHE = {}


def kernel(L0_attn_hidden_states, L0_attn_Wq, L0_attn_Wk, L0_attn_Wv, L0_attn_Wo,
           L0_ffn_W, L0_ffn_bias, L0_ffn_A, L0_ffn_Bf, L0_ffn_coeffs, L0_ffn_scales):
    if "full" not in _BUILD_CACHE:
        _BUILD_CACHE["full"] = build_program()
    nc = _BUILD_CACHE["full"]
    cns = host_consts()
    base = dict(
        wq=np.ascontiguousarray(L0_attn_Wq, np.float32),
        wk=np.ascontiguousarray(L0_attn_Wk, np.float32),
        wv=np.ascontiguousarray(L0_attn_Wv, np.float32),
        wo=np.ascontiguousarray(L0_attn_Wo, np.float32),
        wf=np.ascontiguousarray(L0_ffn_W, np.float32),
        bias=np.ascontiguousarray(L0_ffn_bias, np.float32).reshape(1, HID),
        af=np.ascontiguousarray(L0_ffn_A, np.float32).reshape(256, HID),
        bfm=np.ascontiguousarray(L0_ffn_Bf, np.float32),
        coef=np.ascontiguousarray(np.repeat(
            np.asarray(L0_ffn_coeffs, np.float32).reshape(2, 8).T, 16, axis=0)),
        scal=np.ascontiguousarray(np.repeat(
            np.asarray(L0_ffn_scales, np.float32).reshape(2, 8).T, 16, axis=0)),
        ceps=host_ceps(),
        **cns,
    )
    xs = np.ascontiguousarray(L0_attn_hidden_states, np.float32)
    in_maps = [dict(base, x=np.ascontiguousarray(xs[i])) for i in range(8)]
    res = run_bass_kernel_spmd(nc, in_maps, core_ids=list(range(8)))
    return np.stack([np.asarray(res.results[i]["out"], np.float32)
                     for i in range(8)], axis=0)
